# version 08 (final)
# baseline (speedup 1.0000x reference)
import numpy as np
import ml_dtypes
from contextlib import ExitStack
import concourse.bass as bass
import concourse.mybir as mybir
from concourse.bass_utils import run_bass_kernel_spmd

F32 = mybir.dt.float32
BF16 = mybir.dt.bfloat16
I32 = mybir.dt.int32
AF = mybir.ActivationFunctionType
ALU = mybir.AluOpType
BF16NP = ml_dtypes.bfloat16

D = 4096
TOK = 1024
NCH = 32
II = 11008
NIT = 86
EPS = 1e-5
INV_D = 1.0 / D


class Script:

    ENGS = ("gpsimd", "sync", "tensor", "vector", "scalar")

    def __init__(self):
        self.s = {k: [] for k in self.ENGS}

    def em(self, eng, f):
        self.s[eng].append(f)

    def play(self, nc):
        with nc.Block() as block:
            for name in self.ENGS:
                fs = self.s[name]
                if not fs:
                    continue
                def runner(e, fs=fs):
                    for f in fs:
                        f(e)
                getattr(block, name)(runner)


class Ctx:

    def __init__(self, nc, stack):
        self.nc = nc
        self.S = {}
        self.C = {}
        for nm in ("gld", "sld", "cld", "pe", "ve", "se"):
            self.S[nm] = stack.enter_context(nc.semaphore(nm))
            self.C[nm] = 0

    def dma(self, ph, qeng, sem, dst, src):
        self.C[sem] += 16
        ph.em(qeng, lambda e, d=dst, s=src, S=self.S[sem]: e.dma_start(d, s).then_inc(S, 16))
        return self.C[sem]

    def wait(self, ph, eng, sem, v):
        if v <= 0:
            return
        ph.em(eng, lambda e, S=self.S[sem], vv=v: e.wait_ge(S, vv))

    def inc(self, ph, eng, sem, f):
        self.C[sem] += 1
        def g(e, f=f, S=self.S[sem]):
            f(e).then_inc(S, 1)
        ph.em(eng, g)
        return self.C[sem]


def build(nc):
    xT = nc.dram_tensor("xT", [D, TOK], F32, kind="ExternalInput")
    w1 = nc.dram_tensor("w1", [D, D], F32, kind="ExternalInput")
    w2T = nc.dram_tensor("w2T", [D, D], F32, kind="ExternalInput")
    mask = nc.dram_tensor("mask", [D, D], F32, kind="ExternalInput")
    b2t = nc.dram_tensor("b2t", [128, NCH], F32, kind="ExternalInput")
    qg = nc.dram_tensor("qg", [II, D], I32, kind="ExternalInput")
    qu = nc.dram_tensor("qu", [II, D], I32, kind="ExternalInput")
    qd = nc.dram_tensor("qd", [D, II], I32, kind="ExternalInput")
    sg = nc.dram_tensor("sg", [128, NIT * 32], F32, kind="ExternalInput")
    zg = nc.dram_tensor("zg", [128, NIT * 32], F32, kind="ExternalInput")
    su = nc.dram_tensor("su", [128, NIT * 32], F32, kind="ExternalInput")
    zu = nc.dram_tensor("zu", [128, NIT * 32], F32, kind="ExternalInput")
    sd = nc.dram_tensor("sd", [128, NCH * NIT], F32, kind="ExternalInput")
    zd = nc.dram_tensor("zd", [128, NCH * NIT], F32, kind="ExternalInput")
    ident = nc.dram_tensor("ident", [128, 128], BF16, kind="ExternalInput")
    onesc = nc.dram_tensor("onesc", [128, 1], BF16, kind="ExternalInput")
    onesr = nc.dram_tensor("onesr", [1, 128], BF16, kind="ExternalInput")
    fct = nc.dram_tensor("fct", [1, 2], F32, kind="ExternalInput")
    outT = nc.dram_tensor("outT", [D, TOK], F32, kind="ExternalOutput")

    A_hbm = nc.dram_tensor("A_hbm", [D, D], BF16)
    h_hbm = nc.dram_tensor("h_hbm", [II, TOK], BF16)
    x1_hbm = nc.dram_tensor("x1_hbm", [D, TOK], BF16)
    wgT_hbm = nc.dram_tensor("wgT_hbm", [NIT, 128, D], BF16)
    wuT_hbm = nc.dram_tensor("wuT_hbm", [NIT, 128, D], BF16)
    wdT_hbm = nc.dram_tensor("wdT_hbm", [NCH, 128, II], BF16)

    with ExitStack() as octx:
        oe = octx.enter_context
        cx = Ctx(nc, octx)
        C, dma, wait, inc = cx.C, cx.dma, cx.wait, cx.inc

        idt = oe(nc.sbuf_tensor("idt", [128, 128], BF16))
        onc = oe(nc.sbuf_tensor("onc", [128, 1], BF16))
        onr = oe(nc.sbuf_tensor("onr", [1, 128], BF16))
        b2s = oe(nc.sbuf_tensor("b2s", [128, NCH], F32))
        stats = oe(nc.sbuf_tensor("stats", [1, 4 * TOK], F32))
        ST_M, ST_T, ST_R, ST_MR = 0, TOK, 2 * TOK, 3 * TOK
        fcs = oe(nc.sbuf_tensor("fcs", [1, 2], F32))
        rb = oe(nc.sbuf_tensor("rb", [128, TOK], BF16))
        mrb = oe(nc.sbuf_tensor("mrb", [128, TOK], BF16))

        ph = Script()
        dma(ph, "sync", "sld", idt[:, :], ident[:, :])
        dma(ph, "sync", "sld", onc[:, :], onesc[:, :])
        dma(ph, "sync", "sld", onr[:, :], onesr[:, :])
        dma(ph, "sync", "sld", b2s[:, :], b2t[:, :])
        dma(ph, "sync", "sld", fcs[:, :], fct[:, :])
        wait(ph, "sync", "sld", C["sld"])
        ph.play(nc)

        with ExitStack() as ctx:
            e1 = ctx.enter_context
            w1h = e1(nc.sbuf_tensor("w1h", [128, NCH * 1024], BF16))
            w2b = e1(nc.sbuf_tensor("w2b", [128, 2 * NCH * 512], BF16))
            mkb = e1(nc.sbuf_tensor("mkb", [128, 4 * 512], F32))
            aob = e1(nc.sbuf_tensor("aob", [128, 2 * 512], BF16))
            pA = [e1(nc.psum_tensor(f"pA{i}", [128, 512], F32)) for i in range(2)]
            ph = Script()
            pe0, ve0, cld0 = C["pe"], C["ve"], C["cld"]
            k = 0
            for sh in range(4):
                if sh >= 1:
                    wait(ph, "gpsimd", "pe", C["pe"])
                for c in range(NCH):
                    dma(ph, "gpsimd", "gld", w1h[:, c * 1024:(c + 1) * 1024],
                        w1[c * 128:(c + 1) * 128, sh * 1024:(sh + 1) * 1024])
                w1_ld = C["gld"]
                for tb in range(8):
                    B = sh * 8 + tb
                    bslot = B % 2
                    if B >= 2:
                        wait(ph, "gpsimd", "pe", pe0 + (B - 1) * 8)
                    dma(ph, "gpsimd", "gld",
                        w2b[:, bslot * NCH * 512:(bslot + 1) * NCH * 512].rearrange("p (c j) -> p c j", j=512),
                        w2T[:, tb * 512:(tb + 1) * 512].rearrange("(c p) j -> p c j", p=128))
                    w2_ld = C["gld"]
                    for st in range(8):
                        sgl = sh * 8 + st
                        if k >= 4:
                            wait(ph, "sync", "ve", ve0 + k - 3)
                        dma(ph, "sync", "sld",
                            mkb[:, (k % 4) * 512:(k % 4 + 1) * 512],
                            mask[sgl * 128:(sgl + 1) * 128, tb * 512:(tb + 1) * 512])
                        mk_ld = C["sld"]
                        wait(ph, "tensor", "gld", max(w1_ld, w2_ld))
                        if k >= 2:
                            wait(ph, "tensor", "ve", ve0 + k - 1)
                        for dc in range(NCH):
                            mm = lambda t, dc=dc, k=k, st=st, bslot=bslot: t.matmul(
                                pA[k % 2][:, :],
                                w1h[:, dc * 1024 + st * 128: dc * 1024 + (st + 1) * 128],
                                w2b[:, bslot * NCH * 512 + dc * 512: bslot * NCH * 512 + (dc + 1) * 512],
                                start=(dc == 0), stop=(dc == NCH - 1))
                            if dc == NCH - 1:
                                inc(ph, "tensor", "pe", mm)
                            else:
                                ph.em("tensor", mm)
                        mm_done = C["pe"]
                        wait(ph, "vector", "pe", mm_done)
                        wait(ph, "vector", "sld", mk_ld)
                        if k >= 2:
                            wait(ph, "vector", "cld", cld0 + 16 * (k - 1))
                        inc(ph, "vector", "ve",
                            lambda v, k=k: v.tensor_tensor(
                                aob[:, (k % 2) * 512:(k % 2 + 1) * 512],
                                pA[k % 2][:, :], mkb[:, (k % 4) * 512:(k % 4 + 1) * 512], ALU.mult))
                        wait(ph, "scalar", "ve", C["ve"])
                        dma(ph, "scalar", "cld",
                            A_hbm[sgl * 128:(sgl + 1) * 128, tb * 512:(tb + 1) * 512],
                            aob[:, (k % 2) * 512:(k % 2 + 1) * 512])
                        k += 1
            wait(ph, "scalar", "cld", C["cld"])
            ph.play(nc)

        R1 = oe(nc.sbuf_tensor("R1", [128, 16384], BF16))
        R2 = oe(nc.sbuf_tensor("R2", [128, NCH * TOK], BF16))
        R3 = oe(nc.sbuf_tensor("R3", [128, NCH * TOK], BF16))
        SM = oe(nc.sbuf_tensor("SMB", [128, 3 * TOK], BF16))
        XC0, XC1, SQ = 0, TOK, 2 * TOK
        SFo = oe(nc.sbuf_tensor("SFo", [128, 2 * TOK], F32))

        def ln_finalize(ph, psum_s, psum_q, pb1, pb2):
            inc(ph, "vector", "ve",
                lambda v: v.tensor_scalar(stats[0:1, ST_M:ST_M + TOK], psum_s, INV_D, None, ALU.mult))
            inc(ph, "vector", "ve",
                lambda v: v.tensor_scalar(stats[0:1, ST_T:ST_T + TOK], psum_q, INV_D, None, ALU.mult))
            finalize_rows(ph, pb1, pb2)

        def finalize_rows(ph, pb1, pb2):
            vm = C["ve"]
            wait(ph, "scalar", "ve", vm)
            inc(ph, "scalar", "se",
                lambda s: s.activation(stats[0:1, ST_R:ST_R + TOK], stats[0:1, ST_M:ST_M + TOK], AF.Square))
            wait(ph, "vector", "se", C["se"])
            inc(ph, "vector", "ve",
                lambda v: v.tensor_tensor(stats[0:1, ST_T:ST_T + TOK], stats[0:1, ST_T:ST_T + TOK],
                                          stats[0:1, ST_R:ST_R + TOK], ALU.subtract))
            wait(ph, "scalar", "ve", C["ve"])
            ph.em("scalar", lambda s: s.activation(stats[0:1, ST_MR:ST_MR + TOK],
                                                   stats[0:1, ST_T:ST_T + TOK], AF.Ln, bias=fcs[0:1, 0:1]))
            inc(ph, "scalar", "se",
                lambda s: s.activation(stats[0:1, ST_R:ST_R + TOK], stats[0:1, ST_MR:ST_MR + TOK],
                                       AF.Exp, scale=fcs[0:1, 1:2]))
            wait(ph, "vector", "se", C["se"])
            ph.em("vector", lambda v: v.tensor_tensor(stats[0:1, ST_MR:ST_MR + TOK],
                                                      stats[0:1, ST_M:ST_M + TOK],
                                                      stats[0:1, ST_R:ST_R + TOK], ALU.mult))
            ph.em("vector", lambda v: v.tensor_copy(rb[0:1, :], stats[0:1, ST_R:ST_R + TOK]))
            inc(ph, "vector", "ve", lambda v: v.tensor_copy(mrb[0:1, :], stats[0:1, ST_MR:ST_MR + TOK]))
            wait(ph, "tensor", "ve", C["ve"])
            for half in range(2):
                ph.em("tensor", lambda t, h=half: t.matmul(
                    pb1[:, h * 512:(h + 1) * 512], onr[0:1, :], rb[0:1, h * 512:(h + 1) * 512],
                    start=True, stop=True))
                mmf = lambda t, h=half: t.matmul(
                    pb2[:, h * 512:(h + 1) * 512], onr[0:1, :], mrb[0:1, h * 512:(h + 1) * 512],
                    start=True, stop=True)
                if half == 1:
                    inc(ph, "tensor", "pe", mmf)
                else:
                    ph.em("tensor", mmf)
            wait(ph, "scalar", "pe", C["pe"])
            ph.em("scalar", lambda s: s.copy(rb[:, :], pb1[:, :]))
            inc(ph, "scalar", "se", lambda s: s.copy(mrb[:, :], pb2[:, :]))
            return C["se"]

        with ExitStack() as ctx:
            e2 = ctx.enter_context
            pS = e2(nc.psum_tensor("pSa", [1, TOK], F32))
            pQ = e2(nc.psum_tensor("pQa", [1, TOK], F32))
            ph = Script()
            pe0 = C["pe"]
            for c in range(NCH):
                slot = XC0 if c % 2 == 0 else XC1
                if c >= 2:
                    wait(ph, "gpsimd", "pe", pe0 + c - 1)
                ld = dma(ph, "gpsimd", "gld", SM[:, slot:slot + TOK], xT[c * 128:(c + 1) * 128, :])
                wait(ph, "vector", "gld", ld)
                if c >= 1:
                    wait(ph, "vector", "pe", C["pe"])
                inc(ph, "vector", "ve",
                    lambda v, s=slot: v.tensor_tensor(SM[:, SQ:SQ + TOK], SM[:, s:s + TOK],
                                                      SM[:, s:s + TOK], ALU.mult))
                wait(ph, "tensor", "ve", C["ve"])
                for half in range(2):
                    ph.em("tensor", lambda t, c=c, s=slot, h=half: t.matmul(
                        pS[0:1, h * 512:(h + 1) * 512], onc[:, :],
                        SM[:, s + h * 512:s + (h + 1) * 512],
                        start=(c == 0), stop=(c == NCH - 1)))
                    mmq = lambda t, c=c, h=half: t.matmul(
                        pQ[0:1, h * 512:(h + 1) * 512], onc[:, :],
                        SM[:, SQ + h * 512:SQ + (h + 1) * 512],
                        start=(c == 0), stop=(c == NCH - 1))
                    if half == 1:
                        inc(ph, "tensor", "pe", mmq)
                    else:
                        ph.em("tensor", mmq)
            wait(ph, "vector", "pe", C["pe"])
            ph.play(nc)
            ph = Script()
            with ExitStack() as c2:
                pB1 = c2.enter_context(nc.psum_tensor("pB1a", [128, TOK], F32))
                pB2 = c2.enter_context(nc.psum_tensor("pB2a", [128, TOK], F32))
                rbrdy = 0
                ln_finalize(ph, pS[0:1, :], pQ[0:1, :], pB1, pB2)
                rbrdy = C["se"]
                ve0 = C["ve"]
                for c in range(NCH):
                    slot = XC0 if c % 2 == 0 else XC1
                    if c >= 2:
                        wait(ph, "gpsimd", "ve", ve0 + c - 1)
                    ld = dma(ph, "gpsimd", "gld", SM[:, slot:slot + TOK], xT[c * 128:(c + 1) * 128, :])
                    wait(ph, "vector", "gld", ld)
                    if c == 0:
                        wait(ph, "vector", "se", rbrdy)
                    ph.em("vector", lambda v, s=slot: v.tensor_tensor(
                        SM[:, SQ:SQ + TOK], SM[:, s:s + TOK], rb[:, :], ALU.mult))
                    inc(ph, "vector", "ve",
                        lambda v, c=c: v.tensor_tensor(R2[:, c * TOK:(c + 1) * TOK],
                                                       SM[:, SQ:SQ + TOK], mrb[:, :], ALU.subtract))
                wait(ph, "vector", "ve", C["ve"])
                ph.play(nc)

        with ExitStack() as ctx:
            e3 = ctx.enter_context
            pS = e3(nc.psum_tensor("pSy", [1, TOK], F32))
            pQ = e3(nc.psum_tensor("pQy", [1, TOK], F32))
            with ExitStack() as c3:
                pY = [c3.enter_context(nc.psum_tensor(f"pY{i}", [128, 512], F32)) for i in range(4)]
                ph = Script()
                pe0, se0 = C["pe"], C["se"]
                j = 0
                for tb in range(16):
                    slot = tb % 2
                    if tb >= 2:
                        wait(ph, "sync", "pe", pe0 + (tb - 1) * 8)
                    dma(ph, "sync", "sld",
                        R1[:, slot * 8192:(slot + 1) * 8192].rearrange("p (c j) -> p c j", j=256),
                        A_hbm[:, tb * 256:(tb + 1) * 256].rearrange("(c p) j -> p c j", p=128))
                    a_ld = C["sld"]
                    for tt in range(2):
                        tc = tb * 2 + tt
                        for half in range(2):
                            wait(ph, "tensor", "sld", a_ld)
                            if j >= 4:
                                wait(ph, "tensor", "se", se0 + j - 3)
                            for cch in range(NCH):
                                mmy = lambda t, j=j, s=slot, cch=cch, tt=tt, half=half: t.matmul(
                                    pY[j % 4][:, :],
                                    R1[:, s * 8192 + cch * 256 + tt * 128: s * 8192 + cch * 256 + (tt + 1) * 128],
                                    R2[:, cch * TOK + half * 512: cch * TOK + (half + 1) * 512],
                                    start=(cch == 0), stop=(cch == NCH - 1))
                                if cch == NCH - 1:
                                    inc(ph, "tensor", "pe", mmy)
                                else:
                                    ph.em("tensor", mmy)
                            wait(ph, "scalar", "pe", C["pe"])
                            inc(ph, "scalar", "se",
                                lambda s, j=j, tc=tc, half=half: s.activation(
                                    R3[:, tc * TOK + half * 512: tc * TOK + (half + 1) * 512],
                                    pY[j % 4][:, :], AF.Gelu, bias=b2s[:, tc:tc + 1]))
                            wait(ph, "vector", "se", C["se"])
                            if j >= 1:
                                wait(ph, "vector", "pe", C["pe"])
                            inc(ph, "vector", "ve",
                                lambda v, tc=tc, half=half: v.tensor_tensor(
                                    SM[:, SQ + half * 512:SQ + (half + 1) * 512],
                                    R3[:, tc * TOK + half * 512: tc * TOK + (half + 1) * 512],
                                    R3[:, tc * TOK + half * 512: tc * TOK + (half + 1) * 512], ALU.mult))
                            wait(ph, "tensor", "ve", C["ve"])
                            ph.em("tensor", lambda t, j=j, tc=tc, half=half: t.matmul(
                                pS[0:1, half * 512:(half + 1) * 512], onc[:, :],
                                R3[:, tc * TOK + half * 512: tc * TOK + (half + 1) * 512],
                                start=(j < 2), stop=(j >= 62)))
                            inc(ph, "tensor", "pe",
                                lambda t, j=j, half=half: t.matmul(
                                    pQ[0:1, half * 512:(half + 1) * 512], onc[:, :],
                                    SM[:, SQ + half * 512:SQ + (half + 1) * 512],
                                    start=(j < 2), stop=(j >= 62)))
                            j += 1
                wait(ph, "vector", "pe", C["pe"])
                ph.play(nc)
            ph = Script()
            with ExitStack() as c4:
                pB1 = c4.enter_context(nc.psum_tensor("pB1y", [128, TOK], F32))
                pB2 = c4.enter_context(nc.psum_tensor("pB2y", [128, TOK], F32))
                ln_finalize(ph, pS[0:1, :], pQ[0:1, :], pB1, pB2)
                wait(ph, "scalar", "se", C["se"])
                ph.play(nc)

        with ExitStack() as ctx:
            e5 = ctx.enter_context
            pS = e5(nc.psum_tensor("pSx", [1, TOK], F32))
            pQ = e5(nc.psum_tensor("pQx", [1, TOK], F32))
            ph = Script()
            pe0, ve0, cld0 = C["pe"], C["ve"], C["cld"]
            for c in range(NCH):
                slot = XC0 if c % 2 == 0 else XC1
                if c >= 2:
                    wait(ph, "gpsimd", "ve", ve0 + 2 * (c - 2) + 1)
                ld = dma(ph, "gpsimd", "gld", SM[:, slot:slot + TOK], xT[c * 128:(c + 1) * 128, :])
                wait(ph, "vector", "gld", ld)
                if c >= 2:
                    wait(ph, "vector", "cld", cld0 + 16 * (c - 1))
                if c >= 1:
                    wait(ph, "vector", "pe", pe0 + c)
                ph.em("vector", lambda v, c=c: v.tensor_tensor(
                    SM[:, SQ:SQ + TOK], R3[:, c * TOK:(c + 1) * TOK], rb[:, :], ALU.mult))
                ph.em("vector", lambda v: v.tensor_tensor(
                    SM[:, SQ:SQ + TOK], SM[:, SQ:SQ + TOK], mrb[:, :], ALU.subtract))
                inc(ph, "vector", "ve",
                    lambda v, s=slot, c=c: v.tensor_tensor(
                        R1[:, (c % 2) * TOK:(c % 2) * TOK + TOK], SM[:, SQ:SQ + TOK],
                        SM[:, s:s + TOK], ALU.add))
                wait(ph, "scalar", "ve", C["ve"])
                dma(ph, "scalar", "cld", x1_hbm[c * 128:(c + 1) * 128, :],
                    R1[:, (c % 2) * TOK:(c % 2) * TOK + TOK])
                inc(ph, "vector", "ve",
                    lambda v, c=c: v.tensor_tensor(SM[:, SQ:SQ + TOK],
                                                   R1[:, (c % 2) * TOK:(c % 2) * TOK + TOK],
                                                   R1[:, (c % 2) * TOK:(c % 2) * TOK + TOK], ALU.mult))
                wait(ph, "tensor", "ve", C["ve"])
                for half in range(2):
                    ph.em("tensor", lambda t, c=c, h=half: t.matmul(
                        pS[0:1, h * 512:(h + 1) * 512], onc[:, :],
                        R1[:, (c % 2) * TOK + h * 512:(c % 2) * TOK + (h + 1) * 512],
                        start=(c == 0), stop=(c == NCH - 1)))
                    mmq = lambda t, c=c, h=half: t.matmul(
                        pQ[0:1, h * 512:(h + 1) * 512], onc[:, :],
                        SM[:, SQ + h * 512:SQ + (h + 1) * 512],
                        start=(c == 0), stop=(c == NCH - 1))
                    if half == 1:
                        inc(ph, "tensor", "pe", mmq)
                    else:
                        ph.em("tensor", mmq)
            wait(ph, "vector", "pe", C["pe"])
            wait(ph, "scalar", "cld", C["cld"])
            ph.play(nc)
            ph = Script()
            with ExitStack() as c4:
                pB1 = c4.enter_context(nc.psum_tensor("pB1x", [128, TOK], F32))
                pB2 = c4.enter_context(nc.psum_tensor("pB2x", [128, TOK], F32))
                ln_finalize(ph, pS[0:1, :], pQ[0:1, :], pB1, pB2)
                rbrdy = C["se"]
                ve0 = C["ve"]
                for c in range(NCH):
                    slot = XC0 if c % 2 == 0 else XC1
                    if c >= 2:
                        wait(ph, "sync", "ve", ve0 + c - 1)
                    ld = dma(ph, "sync", "sld", SM[:, slot:slot + TOK], x1_hbm[c * 128:(c + 1) * 128, :])
                    wait(ph, "vector", "sld", ld)
                    if c == 0:
                        wait(ph, "vector", "se", rbrdy)
                    ph.em("vector", lambda v, s=slot: v.tensor_tensor(
                        SM[:, SQ:SQ + TOK], SM[:, s:s + TOK], rb[:, :], ALU.mult))
                    inc(ph, "vector", "ve",
                        lambda v, c=c: v.tensor_tensor(R2[:, c * TOK:(c + 1) * TOK],
                                                       SM[:, SQ:SQ + TOK], mrb[:, :], ALU.subtract))
                wait(ph, "vector", "ve", C["ve"])
                ph.play(nc)

        with ExitStack() as ctx:
            e6 = ctx.enter_context
            sdz = e6(nc.sbuf_tensor("sdz", [128, 4 * 48], F32))
            zdz = e6(nc.sbuf_tensor("zdz", [128, 4 * 48], F32))
            pT = [e6(nc.psum_tensor(f"pT{i}", [128, 512], F32)) for i in range(4)]
            ph = Script()
            SLOT = 5632
            QB, STG = 0, 2 * SLOT

            jobs = ([("g", i, 0) for i in range(NIT)] +
                    [("u", i, 0) for i in range(NIT)] +
                    [("d", i, hj) for i in range(NCH) for hj in range(2)])
            deq_done, tp_done, st_done, ev_done = [], [], [], []
            grp_ev = [0]
            gctr = [0]

            for u, (which, it, hj) in enumerate(jobs):
                uslot = u % 2
                szslot = u % 4
                nsub = NCH if which in ("g", "u") else 43
                qsrc = {"g": qg, "u": qu, "d": qd}[which]
                ssrc = {"g": sg, "u": su, "d": sd}[which]
                zsrc = {"g": zg, "u": zu, "d": zd}[which]
                scol0 = it * 32 if which in ("g", "u") else it * NIT + hj * 43
                qc0 = 0 if which in ("g", "u") else hj * 5504
                nq = nsub * 128
                qb0 = QB + uslot * SLOT
                st0 = STG + uslot * SLOT
                if u >= 4:
                    wait(ph, "sync", "ve", deq_done[u - 4])
                l1 = dma(ph, "sync", "sld", sdz[:, szslot * 48:szslot * 48 + nsub],
                         ssrc[:, scol0:scol0 + nsub])
                l2 = dma(ph, "sync", "sld", zdz[:, szslot * 48:szslot * 48 + nsub],
                         zsrc[:, scol0:scol0 + nsub])
                if u >= 2:
                    wait(ph, "gpsimd", "pe", tp_done[u - 2])
                for hh in range(2):
                    dma(ph, "gpsimd", "gld",
                        R3[:, qb0 + hh * (nq // 2): qb0 + (hh + 1) * (nq // 2)],
                        qsrc[it * 128:(it + 1) * 128, qc0 + hh * (nq // 2): qc0 + (hh + 1) * (nq // 2)])
                q_ld = C["gld"]
                wait(ph, "vector", "sld", max(l1, l2))
                inc(ph, "vector", "ve",
                    lambda v, o=szslot * 48, n=nsub: v.scalar_tensor_tensor(
                        zdz[:, o:o + n], zdz[:, o:o + n], -1.0, sdz[:, o:o + n],
                        ALU.mult, ALU.mult))
                wait(ph, "vector", "gld", q_ld)
                for g2 in range(nsub):
                    tsf = lambda v, g2=g2, o=qb0, so=szslot * 48: v.tensor_scalar(
                        R3[:, o + g2 * 128: o + (g2 + 1) * 128],
                        R3[:, o + g2 * 128: o + (g2 + 1) * 128],
                        sdz[:, so + g2:so + g2 + 1], zdz[:, so + g2:so + g2 + 1],
                        ALU.mult, ALU.add)
                    if g2 == nsub - 1:
                        inc(ph, "vector", "ve", tsf)
                    else:
                        ph.em("vector", tsf)
                deq_done.append(C["ve"])
                if u >= 2:
                    wait(ph, "scalar", "cld", st_done[u - 2])
                kk = 0
                first_grp = True
                while kk < nsub:
                    gsz = min(4, nsub - kk)
                    grp = gctr[0]
                    pt = pT[grp % 4]
                    if grp >= 4:
                        wait(ph, "tensor", "se", grp_ev[grp - 3])
                    if first_grp:
                        wait(ph, "tensor", "ve", deq_done[u])
                        first_grp = False
                    for gi in range(gsz):
                        mmt = lambda t, kk=kk, gi=gi, pt=pt, o=qb0: t.matmul(
                            pt[:, gi * 128:(gi + 1) * 128],
                            R3[:, o + (kk + gi) * 128: o + (kk + gi + 1) * 128],
                            idt[:, :], start=(gi == 0), stop=(gi == gsz - 1))
                        if gi == gsz - 1:
                            inc(ph, "tensor", "pe", mmt)
                        else:
                            ph.em("tensor", mmt)
                    wait(ph, "scalar", "pe", C["pe"])
                    v = inc(ph, "scalar", "se",
                            lambda s, kk=kk, gsz=gsz, pt=pt, o=st0: s.copy(
                                R3[:, o + kk * 128: o + (kk + gsz) * 128],
                                pt[:, 0:gsz * 128]))
                    grp_ev.append(v)
                    gctr[0] += 1
                    kk += gsz
                tp_done.append(C["pe"])
                ev_done.append(C["se"])
                wait(ph, "scalar", "se", ev_done_u := C["se"])
                if which == "g":
                    dst = wgT_hbm[it, :, :]
                elif which == "u":
                    dst = wuT_hbm[it, :, :]
                else:
                    dst = wdT_hbm[it, :, hj * 5504:(hj + 1) * 5504]
                dma(ph, "scalar", "cld", dst, R3[:, st0: st0 + nq])
                st_done.append(C["cld"])
            wait(ph, "scalar", "cld", C["cld"])
            ph.play(nc)

        with ExitStack() as ctx:
            e7 = ctx.enter_context
            pG = [e7(nc.psum_tensor(f"pG{i}", [128, 512], F32)) for i in range(4)]
            pU = [e7(nc.psum_tensor(f"pU{i}", [128, 512], F32)) for i in range(4)]
            ph = Script()
            WGA, WUA, HT, SGT, UBT = 0, 8192, 16384, 18432, 19456
            wg_ld, wu_ld, ev_marks, h_st, h_rdy_l = [], [], [], [], []
            for kk in range(NIT):
                uslot = kk % 2
                if kk >= 2:
                    wait(ph, "sync", "se", ev_marks[kk - 2][0])
                wg_ld.append(dma(ph, "sync", "sld",
                                 R3[:, WGA + uslot * 4096: WGA + (uslot + 1) * 4096],
                                 wgT_hbm[kk, :, :]))
                if kk >= 2:
                    wait(ph, "scalar", "se", ev_marks[kk - 2][1])
                wu_ld.append(dma(ph, "scalar", "cld",
                                 R3[:, WUA + uslot * 4096: WUA + (uslot + 1) * 4096],
                                 wuT_hbm[kk, :, :]))
                wait(ph, "tensor", "sld", wg_ld[kk])
                if kk >= 2:
                    wait(ph, "tensor", "se", ev_marks[kk - 2][2])
                for half in range(2):
                    for cch in range(NCH):
                        mmg = lambda t, u=uslot, half=half, cch=cch, ps=(kk % 2) * 2: t.matmul(
                            pG[ps + half][:, :],
                            R3[:, WGA + u * 4096 + cch * 128: WGA + u * 4096 + (cch + 1) * 128],
                            R2[:, cch * TOK + half * 512: cch * TOK + (half + 1) * 512],
                            start=(cch == 0), stop=(cch == NCH - 1))
                        if half == 1 and cch == NCH - 1:
                            inc(ph, "tensor", "pe", mmg)
                        else:
                            ph.em("tensor", mmg)
                gate_done = C["pe"]
                wait(ph, "tensor", "cld", wu_ld[kk])
                for half in range(2):
                    for cch in range(NCH):
                        mmu = lambda t, u=uslot, half=half, cch=cch, ps=(kk % 2) * 2: t.matmul(
                            pU[ps + half][:, :],
                            R3[:, WUA + u * 4096 + cch * 128: WUA + u * 4096 + (cch + 1) * 128],
                            R2[:, cch * TOK + half * 512: cch * TOK + (half + 1) * 512],
                            start=(cch == 0), stop=(cch == NCH - 1))
                        if half == 1 and cch == NCH - 1:
                            inc(ph, "tensor", "pe", mmu)
                        else:
                            ph.em("tensor", mmu)
                up_done = C["pe"]
                wait(ph, "scalar", "pe", gate_done)
                if kk >= 1:
                    wait(ph, "scalar", "ve", h_rdy_l[kk - 1])
                se_g0 = C["se"]
                for half in range(2):
                    ff = lambda s, half=half, ps=(kk % 2) * 2: s.activation(
                        R3[:, SGT + half * 512:SGT + (half + 1) * 512], pG[ps + half][:, :], AF.Silu)
                    if half == 1:
                        inc(ph, "scalar", "se", ff)
                    else:
                        ph.em("scalar", ff)
                se_g = C["se"]
                wait(ph, "scalar", "pe", up_done)
                for half in range(2):
                    ff = lambda s, half=half, ps=(kk % 2) * 2: s.copy(
                        R3[:, UBT + half * 512:UBT + (half + 1) * 512], pU[ps + half][:, :])
                    if half == 1:
                        inc(ph, "scalar", "se", ff)
                    else:
                        ph.em("scalar", ff)
                se_u = C["se"]
                ev_marks.append((se_g, se_u, se_u))
                wait(ph, "vector", "se", se_u)
                if kk >= 2:
                    wait(ph, "vector", "gld", h_st[kk - 2])
                inc(ph, "vector", "ve",
                    lambda v, u=uslot: v.tensor_tensor(
                        R3[:, HT + u * 1024: HT + u * 1024 + 1024],
                        R3[:, SGT:SGT + 1024], R3[:, UBT:UBT + 1024], ALU.mult))
                h_rdy = C["ve"]
                h_rdy_l.append(h_rdy)
                wait(ph, "gpsimd", "ve", h_rdy)
                h_st.append(dma(ph, "gpsimd", "gld", h_hbm[kk * 128:(kk + 1) * 128, :],
                                R3[:, HT + uslot * 1024: HT + uslot * 1024 + 1024]))
            wait(ph, "gpsimd", "gld", C["gld"])
            ph.play(nc)

        with ExitStack() as ctx:
            e8 = ctx.enter_context
            pD = [e8(nc.psum_tensor(f"pD{i}", [128, 512], F32)) for i in range(8)]
            hsg = [R3[:, 0:11 * 1024], R1[:, 0:11 * 1024]]
            wdb = R3[:, 11264:11264 + 2 * 44 * 128]
            ph = Script()
            SEGS = [(s * 11, 11) for s in range(7)] + [(77, 9)]
            wd_mm = []
            stat_acc = []
            njob = 0
            first = True
            for grp in range(8):
                hts0 = grp * 4
                for seg, (ic0, nseg) in enumerate(SEGS):
                    wslot = seg % 2
                    if njob >= 2:
                        wait(ph, "scalar", "pe", wd_mm[njob - 2])
                    for hi in range(4):
                        dma(ph, "scalar", "cld",
                            wdb[:, wslot * 5632 + hi * nseg * 128: wslot * 5632 + (hi + 1) * nseg * 128],
                            wdT_hbm[hts0 + hi, :, ic0 * 128:(ic0 + nseg) * 128])
                    wl = C["cld"]
                    hslot = njob % 2
                    if njob >= 2:
                        wait(ph, "sync", "pe", wd_mm[njob - 2])
                    hl = dma(ph, "sync", "sld",
                             hsg[hslot][:, 0:nseg * 1024].rearrange("p (c j) -> p c j", j=1024),
                             h_hbm[ic0 * 128:(ic0 + nseg) * 128, :].rearrange("(c p) j -> p c j", p=128))
                    wait(ph, "tensor", "cld", wl)
                    wait(ph, "tensor", "sld", hl)
                    if seg == 0 and grp > 0:
                        wait(ph, "tensor", "ve", stat_acc[grp - 1])
                    for lc in range(nseg):
                        ic = ic0 + lc
                        for hi in range(4):
                            for half in range(2):
                                mmd = lambda t, lc=lc, hi=hi, half=half, ic=ic, w=wslot, n=nseg, hs=hslot: t.matmul(
                                    pD[hi * 2 + half][:, :],
                                    wdb[:, w * 5632 + (hi * n + lc) * 128: w * 5632 + (hi * n + lc + 1) * 128],
                                    hsg[hs][:, lc * 1024 + half * 512: lc * 1024 + (half + 1) * 512],
                                    start=(ic == 0), stop=(ic == NIT - 1))
                                if lc == nseg - 1 and hi == 3 and half == 1:
                                    inc(ph, "tensor", "pe", mmd)
                                else:
                                    ph.em("tensor", mmd)
                    wd_mm.append(C["pe"])
                    njob += 1
                wait(ph, "scalar", "pe", wd_mm[-1])
                for hi in range(4):
                    tc = hts0 + hi
                    for half in range(2):
                        ff = lambda s, tc=tc, hi=hi, half=half: s.copy(
                            R2[:, tc * TOK + half * 512: tc * TOK + (half + 1) * 512],
                            pD[hi * 2 + half][:, :])
                        if hi == 3 and half == 1:
                            inc(ph, "scalar", "se", ff)
                        else:
                            ph.em("scalar", ff)
                f_ev = C["se"]
                wait(ph, "vector", "se", f_ev)
                wait(ph, "tensor", "se", f_ev)
                for hi in range(4):
                    tc = hts0 + hi
                    if hi >= 1:
                        wait(ph, "vector", "pe", C["pe"])
                    inc(ph, "vector", "ve",
                        lambda v, tc=tc: v.tensor_tensor(SM[:, SQ:SQ + TOK],
                                                         R2[:, tc * TOK:(tc + 1) * TOK],
                                                         R2[:, tc * TOK:(tc + 1) * TOK], ALU.mult))
                    wait(ph, "tensor", "ve", C["ve"])
                    for half in range(2):
                        ph.em("tensor", lambda t, tc=tc, hi=hi, half=half: t.matmul(
                            pD[half][0:1, 0:512], onc[:, :],
                            R2[:, tc * TOK + half * 512: tc * TOK + (half + 1) * 512],
                            start=(hi == 0), stop=(hi == 3)))
                        mmq = lambda t, hi=hi, half=half: t.matmul(
                            pD[2 + half][0:1, 0:512], onc[:, :],
                            SM[:, SQ + half * 512:SQ + (half + 1) * 512],
                            start=(hi == 0), stop=(hi == 3))
                        if half == 1:
                            inc(ph, "tensor", "pe", mmq)
                        else:
                            ph.em("tensor", mmq)
                wait(ph, "vector", "pe", C["pe"])
                for half in range(2):
                    for (bank, dstoff) in ((half, ST_M + half * 512), (2 + half, ST_T + half * 512)):
                        op = ALU.add
                        if first:
                            ff = lambda v, b=bank, o=dstoff: v.tensor_copy(
                                stats[0:1, o:o + 512], pD[b][0:1, 0:512])
                        else:
                            ff = lambda v, b=bank, o=dstoff: v.tensor_tensor(
                                stats[0:1, o:o + 512], stats[0:1, o:o + 512], pD[b][0:1, 0:512], ALU.add)
                        if half == 1 and dstoff >= ST_T:
                            inc(ph, "vector", "ve", ff)
                        else:
                            ph.em("vector", ff)
                first = False
                stat_acc.append(C["ve"])
            wait(ph, "vector", "ve", C["ve"])
            ph.play(nc)

        with ExitStack() as ctx:
            e9 = ctx.enter_context
            pB1 = e9(nc.psum_tensor("pB1f", [128, TOK], F32))
            pB2 = e9(nc.psum_tensor("pB2f", [128, TOK], F32))
            ph = Script()
            ph.em("vector", lambda v: v.tensor_scalar(stats[0:1, ST_M:ST_M + TOK],
                                                      stats[0:1, ST_M:ST_M + TOK], INV_D, None, ALU.mult))
            inc(ph, "vector", "ve",
                lambda v: v.tensor_scalar(stats[0:1, ST_T:ST_T + TOK],
                                          stats[0:1, ST_T:ST_T + TOK], INV_D, None, ALU.mult))
            finalize_rows(ph, pB1, pB2)
            rbrdy = C["se"]
            ve0 = C["ve"]
            gld0 = C["gld"]
            for c in range(NCH):
                slot = XC0 if c % 2 == 0 else XC1
                if c >= 2:
                    wait(ph, "sync", "ve", ve0 + c - 1)
                ld = dma(ph, "sync", "sld", SM[:, slot:slot + TOK], x1_hbm[c * 128:(c + 1) * 128, :])
                wait(ph, "vector", "sld", ld)
                if c == 0:
                    wait(ph, "vector", "se", rbrdy)
                if c >= 2:
                    wait(ph, "vector", "gld", gld0 + 16 * (c - 1))
                ph.em("vector", lambda v, c=c: v.tensor_tensor(
                    SM[:, SQ:SQ + TOK], R2[:, c * TOK:(c + 1) * TOK], rb[:, :], ALU.mult))
                ph.em("vector", lambda v: v.tensor_tensor(
                    SM[:, SQ:SQ + TOK], SM[:, SQ:SQ + TOK], mrb[:, :], ALU.subtract))
                inc(ph, "vector", "ve",
                    lambda v, c=c, s=slot: v.tensor_tensor(
                        SFo[:, (c % 2) * TOK:(c % 2 + 1) * TOK], SM[:, SQ:SQ + TOK],
                        SM[:, s:s + TOK], ALU.add))
                wait(ph, "gpsimd", "ve", C["ve"])
                dma(ph, "gpsimd", "gld", outT[c * 128:(c + 1) * 128, :],
                    SFo[:, (c % 2) * TOK:(c % 2 + 1) * TOK])
            wait(ph, "gpsimd", "gld", C["gld"])
            wait(ph, "sync", "gld", C["gld"])
            ph.play(nc)
    return nc


def kernel(L0_attn_x, L0_attn_w1, L0_attn_w2, L0_attn_b2, L0_attn_sparse_mask,
           L0_ffn_gate_qweight, L0_ffn_gate_scales, L0_ffn_gate_zeros,
           L0_ffn_up_qweight, L0_ffn_up_scales, L0_ffn_up_zeros,
           L0_ffn_down_qweight, L0_ffn_down_scales, L0_ffn_down_zeros):
    x = np.ascontiguousarray(np.asarray(L0_attn_x, dtype=np.float32).reshape(8192, D).T)
    w1 = np.ascontiguousarray(np.asarray(L0_attn_w1, dtype=np.float32))
    w2T = np.ascontiguousarray(np.asarray(L0_attn_w2, dtype=np.float32).T)
    maskA = np.ascontiguousarray(np.asarray(L0_attn_sparse_mask, dtype=np.float32))
    b2t = np.ascontiguousarray(np.asarray(L0_attn_b2, dtype=np.float32).reshape(NCH, 128).T)
    qgA = np.ascontiguousarray(np.asarray(L0_ffn_gate_qweight, dtype=np.int32))
    quA = np.ascontiguousarray(np.asarray(L0_ffn_up_qweight, dtype=np.int32))
    qdA = np.ascontiguousarray(np.asarray(L0_ffn_down_qweight, dtype=np.int32))
    relay = lambda a: np.ascontiguousarray(
        np.asarray(a, np.float32).reshape(NIT, 128, 32).transpose(1, 0, 2).reshape(128, NIT * 32))
    relayd = lambda a: np.ascontiguousarray(
        np.asarray(a, np.float32).reshape(NCH, 128, NIT).transpose(1, 0, 2).reshape(128, NCH * NIT))
    ident = np.eye(128, dtype=np.float32).astype(BF16NP)
    onescA = np.ones((128, 1), np.float32).astype(BF16NP)
    onesrA = np.ones((1, 128), np.float32).astype(BF16NP)

    nc = bass.Bass("TRN2", target_bir_lowering=False)
    build(nc)

    base = dict(w1=w1, w2T=w2T, mask=maskA, b2t=b2t, qg=qgA, qu=quA, qd=qdA,
                sg=relay(L0_ffn_gate_scales), zg=relay(L0_ffn_gate_zeros),
                su=relay(L0_ffn_up_scales), zu=relay(L0_ffn_up_zeros),
                sd=relayd(L0_ffn_down_scales), zd=relayd(L0_ffn_down_zeros),
                ident=ident, onesc=onescA, onesr=onesrA,
                fct=np.array([[EPS, -0.5]], dtype=np.float32))
    in_maps = []
    for c in range(8):
        m = dict(base)
        m["xT"] = np.ascontiguousarray(x[:, c * TOK:(c + 1) * TOK])
        in_maps.append(m)
    res = run_bass_kernel_spmd(nc, in_maps, core_ids=list(range(8)))
    shards = [res.results[c]["outT"] for c in range(8)]
    full = np.concatenate(shards, axis=1)
    return np.ascontiguousarray(full.T).reshape(2, D, D).astype(np.float32)


# version 09
# speedup vs baseline: 1.0678x; 1.0678x over previous
import numpy as np
import ml_dtypes
from contextlib import ExitStack
import concourse.bass as bass
import concourse.mybir as mybir
from concourse.bass_utils import run_bass_kernel_spmd

F32 = mybir.dt.float32
BF16 = mybir.dt.bfloat16
I32 = mybir.dt.int32
AF = mybir.ActivationFunctionType
ALU = mybir.AluOpType
BF16NP = ml_dtypes.bfloat16

D = 4096
TOK = 1024
NCH = 32
II = 11008
NIT = 86
EPS = 1e-5
INV_D = 1.0 / D


class Script:

    ENGS = ("gpsimd", "sync", "tensor", "vector", "scalar")

    def __init__(self):
        self.s = {k: [] for k in self.ENGS}

    def em(self, eng, f):
        self.s[eng].append(f)

    def play(self, nc):
        with nc.Block() as block:
            for name in self.ENGS:
                fs = self.s[name]
                if not fs:
                    continue
                def runner(e, fs=fs):
                    for f in fs:
                        f(e)
                getattr(block, name)(runner)


class Ctx:

    def __init__(self, nc, stack):
        self.nc = nc
        self.S = {}
        self.C = {}
        for nm in ("gld", "sld", "cld", "pe", "ve", "se"):
            self.S[nm] = stack.enter_context(nc.semaphore(nm))
            self.C[nm] = 0

    def dma(self, ph, qeng, sem, dst, src):
        self.C[sem] += 16
        ph.em(qeng, lambda e, d=dst, s=src, S=self.S[sem]: e.dma_start(d, s).then_inc(S, 16))
        return self.C[sem]

    def wait(self, ph, eng, sem, v):
        if v <= 0:
            return
        ph.em(eng, lambda e, S=self.S[sem], vv=v: e.wait_ge(S, vv))

    def inc(self, ph, eng, sem, f):
        self.C[sem] += 1
        def g(e, f=f, S=self.S[sem]):
            f(e).then_inc(S, 1)
        ph.em(eng, g)
        return self.C[sem]


def build(nc):
    xT = nc.dram_tensor("xT", [D, TOK], F32, kind="ExternalInput")
    w1 = nc.dram_tensor("w1", [D, D], F32, kind="ExternalInput")
    w2T = nc.dram_tensor("w2T", [D, D], F32, kind="ExternalInput")
    mask = nc.dram_tensor("mask", [D, D], F32, kind="ExternalInput")
    b2t = nc.dram_tensor("b2t", [128, NCH], F32, kind="ExternalInput")
    qg = nc.dram_tensor("qg", [II, D], I32, kind="ExternalInput")
    qu = nc.dram_tensor("qu", [II, D], I32, kind="ExternalInput")
    qd = nc.dram_tensor("qd", [D, II], I32, kind="ExternalInput")
    sg = nc.dram_tensor("sg", [128, NIT * 32], F32, kind="ExternalInput")
    zg = nc.dram_tensor("zg", [128, NIT * 32], F32, kind="ExternalInput")
    su = nc.dram_tensor("su", [128, NIT * 32], F32, kind="ExternalInput")
    zu = nc.dram_tensor("zu", [128, NIT * 32], F32, kind="ExternalInput")
    sd = nc.dram_tensor("sd", [128, NCH * NIT], F32, kind="ExternalInput")
    zd = nc.dram_tensor("zd", [128, NCH * NIT], F32, kind="ExternalInput")
    ident = nc.dram_tensor("ident", [128, 128], BF16, kind="ExternalInput")
    onesc = nc.dram_tensor("onesc", [128, 1], BF16, kind="ExternalInput")
    onesr = nc.dram_tensor("onesr", [1, 128], BF16, kind="ExternalInput")
    fct = nc.dram_tensor("fct", [1, 2], F32, kind="ExternalInput")
    outT = nc.dram_tensor("outT", [D, TOK], F32, kind="ExternalOutput")

    A_hbm = nc.dram_tensor("A_hbm", [D, D], BF16)
    h_hbm = nc.dram_tensor("h_hbm", [II, TOK], BF16)
    x1_hbm = nc.dram_tensor("x1_hbm", [D, TOK], BF16)
    wgT_hbm = nc.dram_tensor("wgT_hbm", [NIT, 128, D], BF16)
    wuT_hbm = nc.dram_tensor("wuT_hbm", [NIT, 128, D], BF16)
    wdT_hbm = nc.dram_tensor("wdT_hbm", [NCH, 128, II], BF16)

    with ExitStack() as octx:
        oe = octx.enter_context
        cx = Ctx(nc, octx)
        C, dma, wait, inc = cx.C, cx.dma, cx.wait, cx.inc

        idt = oe(nc.sbuf_tensor("idt", [128, 128], BF16))
        onc = oe(nc.sbuf_tensor("onc", [128, 1], BF16))
        onr = oe(nc.sbuf_tensor("onr", [1, 128], BF16))
        b2s = oe(nc.sbuf_tensor("b2s", [128, NCH], F32))
        stats = oe(nc.sbuf_tensor("stats", [1, 4 * TOK], F32))
        ST_M, ST_T, ST_R, ST_MR = 0, TOK, 2 * TOK, 3 * TOK
        fcs = oe(nc.sbuf_tensor("fcs", [1, 2], F32))
        rb = oe(nc.sbuf_tensor("rb", [128, TOK], BF16))
        mrb = oe(nc.sbuf_tensor("mrb", [128, TOK], BF16))

        ph = Script()
        dma(ph, "sync", "sld", idt[:, :], ident[:, :])
        dma(ph, "sync", "sld", onc[:, :], onesc[:, :])
        dma(ph, "sync", "sld", onr[:, :], onesr[:, :])
        dma(ph, "sync", "sld", b2s[:, :], b2t[:, :])
        dma(ph, "sync", "sld", fcs[:, :], fct[:, :])
        wait(ph, "sync", "sld", C["sld"])
        ph.play(nc)

        with ExitStack() as ctx:
            e1 = ctx.enter_context
            w1h = e1(nc.sbuf_tensor("w1h", [128, NCH * 1024], BF16))
            w2b = e1(nc.sbuf_tensor("w2b", [128, 2 * NCH * 512], BF16))
            mkb = e1(nc.sbuf_tensor("mkb", [128, 4 * 512], F32))
            aob = e1(nc.sbuf_tensor("aob", [128, 2 * 512], BF16))
            pA = [e1(nc.psum_tensor(f"pA{i}", [128, 512], F32)) for i in range(2)]
            ph = Script()
            pe0, ve0, cld0 = C["pe"], C["ve"], C["cld"]
            k = 0
            for sh in range(4):
                if sh >= 1:
                    wait(ph, "gpsimd", "pe", C["pe"])
                for c in range(NCH):
                    dma(ph, "gpsimd", "gld", w1h[:, c * 1024:(c + 1) * 1024],
                        w1[c * 128:(c + 1) * 128, sh * 1024:(sh + 1) * 1024])
                w1_ld = C["gld"]
                for tb in range(8):
                    B = sh * 8 + tb
                    bslot = B % 2
                    if B >= 2:
                        wait(ph, "gpsimd", "pe", pe0 + (B - 1) * 8)
                    dma(ph, "gpsimd", "gld",
                        w2b[:, bslot * NCH * 512:(bslot + 1) * NCH * 512].rearrange("p (c j) -> p c j", j=512),
                        w2T[:, tb * 512:(tb + 1) * 512].rearrange("(c p) j -> p c j", p=128))
                    w2_ld = C["gld"]
                    for st in range(8):
                        sgl = sh * 8 + st
                        if k >= 4:
                            wait(ph, "sync", "ve", ve0 + k - 3)
                        dma(ph, "sync", "sld",
                            mkb[:, (k % 4) * 512:(k % 4 + 1) * 512],
                            mask[sgl * 128:(sgl + 1) * 128, tb * 512:(tb + 1) * 512])
                        mk_ld = C["sld"]
                        wait(ph, "tensor", "gld", max(w1_ld, w2_ld))
                        if k >= 2:
                            wait(ph, "tensor", "ve", ve0 + k - 1)
                        for dc in range(NCH):
                            mm = lambda t, dc=dc, k=k, st=st, bslot=bslot: t.matmul(
                                pA[k % 2][:, :],
                                w1h[:, dc * 1024 + st * 128: dc * 1024 + (st + 1) * 128],
                                w2b[:, bslot * NCH * 512 + dc * 512: bslot * NCH * 512 + (dc + 1) * 512],
                                start=(dc == 0), stop=(dc == NCH - 1))
                            if dc == NCH - 1:
                                inc(ph, "tensor", "pe", mm)
                            else:
                                ph.em("tensor", mm)
                        mm_done = C["pe"]
                        wait(ph, "vector", "pe", mm_done)
                        wait(ph, "vector", "sld", mk_ld)
                        if k >= 2:
                            wait(ph, "vector", "cld", cld0 + 16 * (k - 1))
                        inc(ph, "vector", "ve",
                            lambda v, k=k: v.tensor_tensor(
                                aob[:, (k % 2) * 512:(k % 2 + 1) * 512],
                                pA[k % 2][:, :], mkb[:, (k % 4) * 512:(k % 4 + 1) * 512], ALU.mult))
                        wait(ph, "scalar", "ve", C["ve"])
                        dma(ph, "scalar", "cld",
                            A_hbm[sgl * 128:(sgl + 1) * 128, tb * 512:(tb + 1) * 512],
                            aob[:, (k % 2) * 512:(k % 2 + 1) * 512])
                        k += 1
            wait(ph, "scalar", "cld", C["cld"])
            ph.play(nc)

        R1 = oe(nc.sbuf_tensor("R1", [128, 16384], BF16))
        R2 = oe(nc.sbuf_tensor("R2", [128, NCH * TOK], BF16))
        R3 = oe(nc.sbuf_tensor("R3", [128, NCH * TOK], BF16))
        SM = oe(nc.sbuf_tensor("SMB", [128, 3 * TOK], BF16))
        XC0, XC1, SQ = 0, TOK, 2 * TOK
        SFo = oe(nc.sbuf_tensor("SFo", [128, 2 * TOK], F32))

        def ln_finalize(ph, psum_s, psum_q, pb1, pb2):
            inc(ph, "vector", "ve",
                lambda v: v.tensor_scalar(stats[0:1, ST_M:ST_M + TOK], psum_s, INV_D, None, ALU.mult))
            inc(ph, "vector", "ve",
                lambda v: v.tensor_scalar(stats[0:1, ST_T:ST_T + TOK], psum_q, INV_D, None, ALU.mult))
            finalize_rows(ph, pb1, pb2)

        def finalize_rows(ph, pb1, pb2):
            vm = C["ve"]
            wait(ph, "scalar", "ve", vm)
            inc(ph, "scalar", "se",
                lambda s: s.activation(stats[0:1, ST_R:ST_R + TOK], stats[0:1, ST_M:ST_M + TOK], AF.Square))
            wait(ph, "vector", "se", C["se"])
            inc(ph, "vector", "ve",
                lambda v: v.tensor_tensor(stats[0:1, ST_T:ST_T + TOK], stats[0:1, ST_T:ST_T + TOK],
                                          stats[0:1, ST_R:ST_R + TOK], ALU.subtract))
            wait(ph, "scalar", "ve", C["ve"])
            ph.em("scalar", lambda s: s.activation(stats[0:1, ST_MR:ST_MR + TOK],
                                                   stats[0:1, ST_T:ST_T + TOK], AF.Ln, bias=fcs[0:1, 0:1]))
            inc(ph, "scalar", "se",
                lambda s: s.activation(stats[0:1, ST_R:ST_R + TOK], stats[0:1, ST_MR:ST_MR + TOK],
                                       AF.Exp, scale=fcs[0:1, 1:2]))
            wait(ph, "vector", "se", C["se"])
            ph.em("vector", lambda v: v.tensor_tensor(stats[0:1, ST_MR:ST_MR + TOK],
                                                      stats[0:1, ST_M:ST_M + TOK],
                                                      stats[0:1, ST_R:ST_R + TOK], ALU.mult))
            ph.em("vector", lambda v: v.tensor_copy(rb[0:1, :], stats[0:1, ST_R:ST_R + TOK]))
            inc(ph, "vector", "ve", lambda v: v.tensor_copy(mrb[0:1, :], stats[0:1, ST_MR:ST_MR + TOK]))
            wait(ph, "tensor", "ve", C["ve"])
            for half in range(2):
                ph.em("tensor", lambda t, h=half: t.matmul(
                    pb1[:, h * 512:(h + 1) * 512], onr[0:1, :], rb[0:1, h * 512:(h + 1) * 512],
                    start=True, stop=True))
                mmf = lambda t, h=half: t.matmul(
                    pb2[:, h * 512:(h + 1) * 512], onr[0:1, :], mrb[0:1, h * 512:(h + 1) * 512],
                    start=True, stop=True)
                if half == 1:
                    inc(ph, "tensor", "pe", mmf)
                else:
                    ph.em("tensor", mmf)
            wait(ph, "scalar", "pe", C["pe"])
            ph.em("scalar", lambda s: s.copy(rb[:, :], pb1[:, :]))
            inc(ph, "scalar", "se", lambda s: s.copy(mrb[:, :], pb2[:, :]))
            return C["se"]

        with ExitStack() as ctx:
            e2 = ctx.enter_context
            pS = e2(nc.psum_tensor("pSa", [1, TOK], F32))
            pQ = e2(nc.psum_tensor("pQa", [1, TOK], F32))
            ph = Script()
            pe0 = C["pe"]
            for c in range(NCH):
                slot = XC0 if c % 2 == 0 else XC1
                if c >= 2:
                    wait(ph, "gpsimd", "pe", pe0 + c - 1)
                ld = dma(ph, "gpsimd", "gld", SM[:, slot:slot + TOK], xT[c * 128:(c + 1) * 128, :])
                wait(ph, "vector", "gld", ld)
                if c >= 1:
                    wait(ph, "vector", "pe", C["pe"])
                inc(ph, "vector", "ve",
                    lambda v, s=slot: v.tensor_tensor(SM[:, SQ:SQ + TOK], SM[:, s:s + TOK],
                                                      SM[:, s:s + TOK], ALU.mult))
                wait(ph, "tensor", "ve", C["ve"])
                for half in range(2):
                    ph.em("tensor", lambda t, c=c, s=slot, h=half: t.matmul(
                        pS[0:1, h * 512:(h + 1) * 512], onc[:, :],
                        SM[:, s + h * 512:s + (h + 1) * 512],
                        start=(c == 0), stop=(c == NCH - 1)))
                    mmq = lambda t, c=c, h=half: t.matmul(
                        pQ[0:1, h * 512:(h + 1) * 512], onc[:, :],
                        SM[:, SQ + h * 512:SQ + (h + 1) * 512],
                        start=(c == 0), stop=(c == NCH - 1))
                    if half == 1:
                        inc(ph, "tensor", "pe", mmq)
                    else:
                        ph.em("tensor", mmq)
            wait(ph, "vector", "pe", C["pe"])
            ph.play(nc)
            ph = Script()
            with ExitStack() as c2:
                pB1 = c2.enter_context(nc.psum_tensor("pB1a", [128, TOK], F32))
                pB2 = c2.enter_context(nc.psum_tensor("pB2a", [128, TOK], F32))
                rbrdy = 0
                ln_finalize(ph, pS[0:1, :], pQ[0:1, :], pB1, pB2)
                rbrdy = C["se"]
                ve0 = C["ve"]
                for c in range(NCH):
                    slot = XC0 if c % 2 == 0 else XC1
                    if c >= 2:
                        wait(ph, "gpsimd", "ve", ve0 + c - 1)
                    ld = dma(ph, "gpsimd", "gld", SM[:, slot:slot + TOK], xT[c * 128:(c + 1) * 128, :])
                    wait(ph, "vector", "gld", ld)
                    if c == 0:
                        wait(ph, "vector", "se", rbrdy)
                    ph.em("vector", lambda v, s=slot: v.tensor_tensor(
                        SM[:, SQ:SQ + TOK], SM[:, s:s + TOK], rb[:, :], ALU.mult))
                    inc(ph, "vector", "ve",
                        lambda v, c=c: v.tensor_tensor(R2[:, c * TOK:(c + 1) * TOK],
                                                       SM[:, SQ:SQ + TOK], mrb[:, :], ALU.subtract))
                wait(ph, "vector", "ve", C["ve"])
                ph.play(nc)

        with ExitStack() as ctx:
            e3 = ctx.enter_context
            pS = e3(nc.psum_tensor("pSy", [1, TOK], F32))
            pQ = e3(nc.psum_tensor("pQy", [1, TOK], F32))
            with ExitStack() as c3:
                pY = [c3.enter_context(nc.psum_tensor(f"pY{i}", [128, 512], F32)) for i in range(4)]
                ph = Script()
                pe0, se0 = C["pe"], C["se"]
                j = 0
                for tb in range(16):
                    slot = tb % 2
                    if tb >= 2:
                        wait(ph, "sync", "pe", pe0 + (tb - 1) * 8)
                    dma(ph, "sync", "sld",
                        R1[:, slot * 8192:(slot + 1) * 8192].rearrange("p (c j) -> p c j", j=256),
                        A_hbm[:, tb * 256:(tb + 1) * 256].rearrange("(c p) j -> p c j", p=128))
                    a_ld = C["sld"]
                    for tt in range(2):
                        tc = tb * 2 + tt
                        for half in range(2):
                            wait(ph, "tensor", "sld", a_ld)
                            if j >= 4:
                                wait(ph, "tensor", "se", se0 + j - 3)
                            for cch in range(NCH):
                                mmy = lambda t, j=j, s=slot, cch=cch, tt=tt, half=half: t.matmul(
                                    pY[j % 4][:, :],
                                    R1[:, s * 8192 + cch * 256 + tt * 128: s * 8192 + cch * 256 + (tt + 1) * 128],
                                    R2[:, cch * TOK + half * 512: cch * TOK + (half + 1) * 512],
                                    start=(cch == 0), stop=(cch == NCH - 1))
                                if cch == NCH - 1:
                                    inc(ph, "tensor", "pe", mmy)
                                else:
                                    ph.em("tensor", mmy)
                            wait(ph, "scalar", "pe", C["pe"])
                            inc(ph, "scalar", "se",
                                lambda s, j=j, tc=tc, half=half: s.activation(
                                    R3[:, tc * TOK + half * 512: tc * TOK + (half + 1) * 512],
                                    pY[j % 4][:, :], AF.Gelu, bias=b2s[:, tc:tc + 1]))
                            wait(ph, "vector", "se", C["se"])
                            if j >= 1:
                                wait(ph, "vector", "pe", C["pe"])
                            inc(ph, "vector", "ve",
                                lambda v, tc=tc, half=half: v.tensor_tensor(
                                    SM[:, SQ + half * 512:SQ + (half + 1) * 512],
                                    R3[:, tc * TOK + half * 512: tc * TOK + (half + 1) * 512],
                                    R3[:, tc * TOK + half * 512: tc * TOK + (half + 1) * 512], ALU.mult))
                            wait(ph, "tensor", "ve", C["ve"])
                            ph.em("tensor", lambda t, j=j, tc=tc, half=half: t.matmul(
                                pS[0:1, half * 512:(half + 1) * 512], onc[:, :],
                                R3[:, tc * TOK + half * 512: tc * TOK + (half + 1) * 512],
                                start=(j < 2), stop=(j >= 62)))
                            inc(ph, "tensor", "pe",
                                lambda t, j=j, half=half: t.matmul(
                                    pQ[0:1, half * 512:(half + 1) * 512], onc[:, :],
                                    SM[:, SQ + half * 512:SQ + (half + 1) * 512],
                                    start=(j < 2), stop=(j >= 62)))
                            j += 1
                wait(ph, "vector", "pe", C["pe"])
                ph.play(nc)
            ph = Script()
            with ExitStack() as c4:
                pB1 = c4.enter_context(nc.psum_tensor("pB1y", [128, TOK], F32))
                pB2 = c4.enter_context(nc.psum_tensor("pB2y", [128, TOK], F32))
                ln_finalize(ph, pS[0:1, :], pQ[0:1, :], pB1, pB2)
                wait(ph, "scalar", "se", C["se"])
                ph.play(nc)

        with ExitStack() as ctx:
            e5 = ctx.enter_context
            pS = e5(nc.psum_tensor("pSx", [1, TOK], F32))
            pQ = e5(nc.psum_tensor("pQx", [1, TOK], F32))
            ph = Script()
            pe0, ve0, cld0 = C["pe"], C["ve"], C["cld"]
            for c in range(NCH):
                slot = XC0 if c % 2 == 0 else XC1
                if c >= 2:
                    wait(ph, "gpsimd", "ve", ve0 + 2 * (c - 2) + 1)
                ld = dma(ph, "gpsimd", "gld", SM[:, slot:slot + TOK], xT[c * 128:(c + 1) * 128, :])
                wait(ph, "vector", "gld", ld)
                if c >= 2:
                    wait(ph, "vector", "cld", cld0 + 16 * (c - 1))
                if c >= 1:
                    wait(ph, "vector", "pe", pe0 + c)
                ph.em("vector", lambda v, c=c: v.tensor_tensor(
                    SM[:, SQ:SQ + TOK], R3[:, c * TOK:(c + 1) * TOK], rb[:, :], ALU.mult))
                ph.em("vector", lambda v: v.tensor_tensor(
                    SM[:, SQ:SQ + TOK], SM[:, SQ:SQ + TOK], mrb[:, :], ALU.subtract))
                inc(ph, "vector", "ve",
                    lambda v, s=slot, c=c: v.tensor_tensor(
                        R1[:, (c % 2) * TOK:(c % 2) * TOK + TOK], SM[:, SQ:SQ + TOK],
                        SM[:, s:s + TOK], ALU.add))
                wait(ph, "scalar", "ve", C["ve"])
                dma(ph, "scalar", "cld", x1_hbm[c * 128:(c + 1) * 128, :],
                    R1[:, (c % 2) * TOK:(c % 2) * TOK + TOK])
                inc(ph, "vector", "ve",
                    lambda v, c=c: v.tensor_tensor(SM[:, SQ:SQ + TOK],
                                                   R1[:, (c % 2) * TOK:(c % 2) * TOK + TOK],
                                                   R1[:, (c % 2) * TOK:(c % 2) * TOK + TOK], ALU.mult))
                wait(ph, "tensor", "ve", C["ve"])
                for half in range(2):
                    ph.em("tensor", lambda t, c=c, h=half: t.matmul(
                        pS[0:1, h * 512:(h + 1) * 512], onc[:, :],
                        R1[:, (c % 2) * TOK + h * 512:(c % 2) * TOK + (h + 1) * 512],
                        start=(c == 0), stop=(c == NCH - 1)))
                    mmq = lambda t, c=c, h=half: t.matmul(
                        pQ[0:1, h * 512:(h + 1) * 512], onc[:, :],
                        SM[:, SQ + h * 512:SQ + (h + 1) * 512],
                        start=(c == 0), stop=(c == NCH - 1))
                    if half == 1:
                        inc(ph, "tensor", "pe", mmq)
                    else:
                        ph.em("tensor", mmq)
            wait(ph, "vector", "pe", C["pe"])
            wait(ph, "scalar", "cld", C["cld"])
            ph.play(nc)
            ph = Script()
            with ExitStack() as c4:
                pB1 = c4.enter_context(nc.psum_tensor("pB1x", [128, TOK], F32))
                pB2 = c4.enter_context(nc.psum_tensor("pB2x", [128, TOK], F32))
                ln_finalize(ph, pS[0:1, :], pQ[0:1, :], pB1, pB2)
                rbrdy = C["se"]
                ve0 = C["ve"]
                for c in range(NCH):
                    slot = XC0 if c % 2 == 0 else XC1
                    if c >= 2:
                        wait(ph, "sync", "ve", ve0 + c - 1)
                    ld = dma(ph, "sync", "sld", SM[:, slot:slot + TOK], x1_hbm[c * 128:(c + 1) * 128, :])
                    wait(ph, "vector", "sld", ld)
                    if c == 0:
                        wait(ph, "vector", "se", rbrdy)
                    ph.em("vector", lambda v, s=slot: v.tensor_tensor(
                        SM[:, SQ:SQ + TOK], SM[:, s:s + TOK], rb[:, :], ALU.mult))
                    inc(ph, "vector", "ve",
                        lambda v, c=c: v.tensor_tensor(R2[:, c * TOK:(c + 1) * TOK],
                                                       SM[:, SQ:SQ + TOK], mrb[:, :], ALU.subtract))
                wait(ph, "vector", "ve", C["ve"])
                ph.play(nc)

        with ExitStack() as ctx:
            e6 = ctx.enter_context
            sdz = e6(nc.sbuf_tensor("sdz", [128, 4 * 48], F32))
            zdz = e6(nc.sbuf_tensor("zdz", [128, 4 * 48], F32))
            pT = [e6(nc.psum_tensor(f"pT{i}", [128, 512], F32)) for i in range(4)]
            ph = Script()
            SLOT = 5632
            QB, STG = 0, 2 * SLOT

            jobs = ([("g", i, 0) for i in range(NIT)] +
                    [("u", i, 0) for i in range(NIT)] +
                    [("d", i, hj) for i in range(NCH) for hj in range(2)])
            deq_done, tp_done, st_done, ev_done, deq_se = [], [], [], [], []
            grp_ev = [0]
            gctr = [0]

            for u, (which, it, hj) in enumerate(jobs):
                uslot = u % 2
                szslot = u % 4
                nsub = NCH if which in ("g", "u") else 43
                qsrc = {"g": qg, "u": qu, "d": qd}[which]
                ssrc = {"g": sg, "u": su, "d": sd}[which]
                zsrc = {"g": zg, "u": zu, "d": zd}[which]
                scol0 = it * 32 if which in ("g", "u") else it * NIT + hj * 43
                qc0 = 0 if which in ("g", "u") else hj * 5504
                nq = nsub * 128
                qb0 = QB + uslot * SLOT
                st0 = STG + uslot * SLOT
                if u >= 4:
                    wait(ph, "sync", "ve", deq_done[u - 4])
                    wait(ph, "sync", "se", deq_se[u - 4])
                l1 = dma(ph, "sync", "sld", sdz[:, szslot * 48:szslot * 48 + nsub],
                         ssrc[:, scol0:scol0 + nsub])
                l2 = dma(ph, "sync", "sld", zdz[:, szslot * 48:szslot * 48 + nsub],
                         zsrc[:, scol0:scol0 + nsub])
                if u >= 2:
                    wait(ph, "gpsimd", "pe", tp_done[u - 2])
                for hh in range(2):
                    dma(ph, "gpsimd", "gld",
                        R3[:, qb0 + hh * (nq // 2): qb0 + (hh + 1) * (nq // 2)],
                        qsrc[it * 128:(it + 1) * 128, qc0 + hh * (nq // 2): qc0 + (hh + 1) * (nq // 2)])
                q_ld = C["gld"]
                wait(ph, "vector", "sld", max(l1, l2))
                inc(ph, "vector", "ve",
                    lambda v, o=szslot * 48, n=nsub: v.scalar_tensor_tensor(
                        zdz[:, o:o + n], zdz[:, o:o + n], -1.0, sdz[:, o:o + n],
                        ALU.mult, ALU.mult))
                zs_done = C["ve"]
                wait(ph, "vector", "gld", q_ld)
                wait(ph, "scalar", "gld", q_ld)
                wait(ph, "scalar", "ve", zs_done)
                vlist = [g2 for g2 in range(nsub) if g2 % 4 != 3]
                slist = [g2 for g2 in range(nsub) if g2 % 4 == 3]
                for g2 in slist:
                    sdf = lambda s, g2=g2, o=qb0, so=szslot * 48: s.activation(
                        R3[:, o + g2 * 128: o + (g2 + 1) * 128],
                        R3[:, o + g2 * 128: o + (g2 + 1) * 128],
                        AF.Identity, bias=zdz[:, so + g2:so + g2 + 1],
                        scale=sdz[:, so + g2:so + g2 + 1])
                    if g2 == slist[-1]:
                        inc(ph, "scalar", "se", sdf)
                    else:
                        ph.em("scalar", sdf)
                deq_se.append(C["se"])
                for g2 in vlist:
                    tsf = lambda v, g2=g2, o=qb0, so=szslot * 48: v.tensor_scalar(
                        R3[:, o + g2 * 128: o + (g2 + 1) * 128],
                        R3[:, o + g2 * 128: o + (g2 + 1) * 128],
                        sdz[:, so + g2:so + g2 + 1], zdz[:, so + g2:so + g2 + 1],
                        ALU.mult, ALU.add)
                    if g2 == vlist[-1]:
                        inc(ph, "vector", "ve", tsf)
                    else:
                        ph.em("vector", tsf)
                deq_done.append(C["ve"])
                if u >= 2:
                    wait(ph, "scalar", "cld", st_done[u - 2])
                kk = 0
                first_grp = True
                while kk < nsub:
                    gsz = min(4, nsub - kk)
                    grp = gctr[0]
                    pt = pT[grp % 4]
                    if grp >= 4:
                        wait(ph, "tensor", "se", grp_ev[grp - 3])
                    if first_grp:
                        wait(ph, "tensor", "ve", deq_done[u])
                        wait(ph, "tensor", "se", deq_se[u])
                        first_grp = False
                    for gi in range(gsz):
                        mmt = lambda t, kk=kk, gi=gi, pt=pt, o=qb0: t.matmul(
                            pt[:, gi * 128:(gi + 1) * 128],
                            R3[:, o + (kk + gi) * 128: o + (kk + gi + 1) * 128],
                            idt[:, :], start=(gi == 0), stop=(gi == gsz - 1))
                        if gi == gsz - 1:
                            inc(ph, "tensor", "pe", mmt)
                        else:
                            ph.em("tensor", mmt)
                    wait(ph, "scalar", "pe", C["pe"])
                    v = inc(ph, "scalar", "se",
                            lambda s, kk=kk, gsz=gsz, pt=pt, o=st0: s.copy(
                                R3[:, o + kk * 128: o + (kk + gsz) * 128],
                                pt[:, 0:gsz * 128]))
                    grp_ev.append(v)
                    gctr[0] += 1
                    kk += gsz
                tp_done.append(C["pe"])
                ev_done.append(C["se"])
                wait(ph, "scalar", "se", ev_done_u := C["se"])
                if which == "g":
                    dst = wgT_hbm[it, :, :]
                elif which == "u":
                    dst = wuT_hbm[it, :, :]
                else:
                    dst = wdT_hbm[it, :, hj * 5504:(hj + 1) * 5504]
                dma(ph, "scalar", "cld", dst, R3[:, st0: st0 + nq])
                st_done.append(C["cld"])
            wait(ph, "scalar", "cld", C["cld"])
            ph.play(nc)

        with ExitStack() as ctx:
            e7 = ctx.enter_context
            pG = [e7(nc.psum_tensor(f"pG{i}", [128, 512], F32)) for i in range(4)]
            pU = [e7(nc.psum_tensor(f"pU{i}", [128, 512], F32)) for i in range(4)]
            ph = Script()
            WGA, WUA, HT, SGT, UBT = 0, 8192, 16384, 18432, 19456
            wg_ld, wu_ld, ev_marks, h_st, h_rdy_l = [], [], [], [], []
            for kk in range(NIT):
                uslot = kk % 2
                if kk >= 2:
                    wait(ph, "sync", "se", ev_marks[kk - 2][0])
                wg_ld.append(dma(ph, "sync", "sld",
                                 R3[:, WGA + uslot * 4096: WGA + (uslot + 1) * 4096],
                                 wgT_hbm[kk, :, :]))
                if kk >= 2:
                    wait(ph, "scalar", "se", ev_marks[kk - 2][1])
                wu_ld.append(dma(ph, "scalar", "cld",
                                 R3[:, WUA + uslot * 4096: WUA + (uslot + 1) * 4096],
                                 wuT_hbm[kk, :, :]))
                wait(ph, "tensor", "sld", wg_ld[kk])
                if kk >= 2:
                    wait(ph, "tensor", "se", ev_marks[kk - 2][2])
                for half in range(2):
                    for cch in range(NCH):
                        mmg = lambda t, u=uslot, half=half, cch=cch, ps=(kk % 2) * 2: t.matmul(
                            pG[ps + half][:, :],
                            R3[:, WGA + u * 4096 + cch * 128: WGA + u * 4096 + (cch + 1) * 128],
                            R2[:, cch * TOK + half * 512: cch * TOK + (half + 1) * 512],
                            start=(cch == 0), stop=(cch == NCH - 1))
                        if half == 1 and cch == NCH - 1:
                            inc(ph, "tensor", "pe", mmg)
                        else:
                            ph.em("tensor", mmg)
                gate_done = C["pe"]
                wait(ph, "tensor", "cld", wu_ld[kk])
                for half in range(2):
                    for cch in range(NCH):
                        mmu = lambda t, u=uslot, half=half, cch=cch, ps=(kk % 2) * 2: t.matmul(
                            pU[ps + half][:, :],
                            R3[:, WUA + u * 4096 + cch * 128: WUA + u * 4096 + (cch + 1) * 128],
                            R2[:, cch * TOK + half * 512: cch * TOK + (half + 1) * 512],
                            start=(cch == 0), stop=(cch == NCH - 1))
                        if half == 1 and cch == NCH - 1:
                            inc(ph, "tensor", "pe", mmu)
                        else:
                            ph.em("tensor", mmu)
                up_done = C["pe"]
                wait(ph, "scalar", "pe", gate_done)
                if kk >= 1:
                    wait(ph, "scalar", "ve", h_rdy_l[kk - 1])
                se_g0 = C["se"]
                for half in range(2):
                    ff = lambda s, half=half, ps=(kk % 2) * 2: s.activation(
                        R3[:, SGT + half * 512:SGT + (half + 1) * 512], pG[ps + half][:, :], AF.Silu)
                    if half == 1:
                        inc(ph, "scalar", "se", ff)
                    else:
                        ph.em("scalar", ff)
                se_g = C["se"]
                wait(ph, "scalar", "pe", up_done)
                for half in range(2):
                    ff = lambda s, half=half, ps=(kk % 2) * 2: s.copy(
                        R3[:, UBT + half * 512:UBT + (half + 1) * 512], pU[ps + half][:, :])
                    if half == 1:
                        inc(ph, "scalar", "se", ff)
                    else:
                        ph.em("scalar", ff)
                se_u = C["se"]
                ev_marks.append((se_g, se_u, se_u))
                wait(ph, "vector", "se", se_u)
                if kk >= 2:
                    wait(ph, "vector", "gld", h_st[kk - 2])
                inc(ph, "vector", "ve",
                    lambda v, u=uslot: v.tensor_tensor(
                        R3[:, HT + u * 1024: HT + u * 1024 + 1024],
                        R3[:, SGT:SGT + 1024], R3[:, UBT:UBT + 1024], ALU.mult))
                h_rdy = C["ve"]
                h_rdy_l.append(h_rdy)
                wait(ph, "gpsimd", "ve", h_rdy)
                h_st.append(dma(ph, "gpsimd", "gld", h_hbm[kk * 128:(kk + 1) * 128, :],
                                R3[:, HT + uslot * 1024: HT + uslot * 1024 + 1024]))
            wait(ph, "gpsimd", "gld", C["gld"])
            ph.play(nc)

        with ExitStack() as ctx:
            e8 = ctx.enter_context
            pD = [e8(nc.psum_tensor(f"pD{i}", [128, 512], F32)) for i in range(8)]
            hsg = [R3[:, 0:11 * 1024], R1[:, 0:11 * 1024]]
            wdb = R3[:, 11264:11264 + 2 * 44 * 128]
            ph = Script()
            SEGS = [(s * 11, 11) for s in range(7)] + [(77, 9)]
            wd_mm = []
            stat_acc = []
            njob = 0
            first = True
            for grp in range(8):
                hts0 = grp * 4
                for seg, (ic0, nseg) in enumerate(SEGS):
                    wslot = seg % 2
                    if njob >= 2:
                        wait(ph, "scalar", "pe", wd_mm[njob - 2])
                    for hi in range(4):
                        dma(ph, "scalar", "cld",
                            wdb[:, wslot * 5632 + hi * nseg * 128: wslot * 5632 + (hi + 1) * nseg * 128],
                            wdT_hbm[hts0 + hi, :, ic0 * 128:(ic0 + nseg) * 128])
                    wl = C["cld"]
                    hslot = njob % 2
                    if njob >= 2:
                        wait(ph, "sync", "pe", wd_mm[njob - 2])
                    hl = dma(ph, "sync", "sld",
                             hsg[hslot][:, 0:nseg * 1024].rearrange("p (c j) -> p c j", j=1024),
                             h_hbm[ic0 * 128:(ic0 + nseg) * 128, :].rearrange("(c p) j -> p c j", p=128))
                    wait(ph, "tensor", "cld", wl)
                    wait(ph, "tensor", "sld", hl)
                    if seg == 0 and grp > 0:
                        wait(ph, "tensor", "ve", stat_acc[grp - 1])
                    for lc in range(nseg):
                        ic = ic0 + lc
                        for hi in range(4):
                            for half in range(2):
                                mmd = lambda t, lc=lc, hi=hi, half=half, ic=ic, w=wslot, n=nseg, hs=hslot: t.matmul(
                                    pD[hi * 2 + half][:, :],
                                    wdb[:, w * 5632 + (hi * n + lc) * 128: w * 5632 + (hi * n + lc + 1) * 128],
                                    hsg[hs][:, lc * 1024 + half * 512: lc * 1024 + (half + 1) * 512],
                                    start=(ic == 0), stop=(ic == NIT - 1))
                                if lc == nseg - 1 and hi == 3 and half == 1:
                                    inc(ph, "tensor", "pe", mmd)
                                else:
                                    ph.em("tensor", mmd)
                    wd_mm.append(C["pe"])
                    njob += 1
                wait(ph, "scalar", "pe", wd_mm[-1])
                for hi in range(4):
                    tc = hts0 + hi
                    for half in range(2):
                        ff = lambda s, tc=tc, hi=hi, half=half: s.copy(
                            R2[:, tc * TOK + half * 512: tc * TOK + (half + 1) * 512],
                            pD[hi * 2 + half][:, :])
                        if hi == 3 and half == 1:
                            inc(ph, "scalar", "se", ff)
                        else:
                            ph.em("scalar", ff)
                f_ev = C["se"]
                wait(ph, "vector", "se", f_ev)
                wait(ph, "tensor", "se", f_ev)
                for hi in range(4):
                    tc = hts0 + hi
                    if hi >= 1:
                        wait(ph, "vector", "pe", C["pe"])
                    inc(ph, "vector", "ve",
                        lambda v, tc=tc: v.tensor_tensor(SM[:, SQ:SQ + TOK],
                                                         R2[:, tc * TOK:(tc + 1) * TOK],
                                                         R2[:, tc * TOK:(tc + 1) * TOK], ALU.mult))
                    wait(ph, "tensor", "ve", C["ve"])
                    for half in range(2):
                        ph.em("tensor", lambda t, tc=tc, hi=hi, half=half: t.matmul(
                            pD[half][0:1, 0:512], onc[:, :],
                            R2[:, tc * TOK + half * 512: tc * TOK + (half + 1) * 512],
                            start=(hi == 0), stop=(hi == 3)))
                        mmq = lambda t, hi=hi, half=half: t.matmul(
                            pD[2 + half][0:1, 0:512], onc[:, :],
                            SM[:, SQ + half * 512:SQ + (half + 1) * 512],
                            start=(hi == 0), stop=(hi == 3))
                        if half == 1:
                            inc(ph, "tensor", "pe", mmq)
                        else:
                            ph.em("tensor", mmq)
                wait(ph, "vector", "pe", C["pe"])
                for half in range(2):
                    for (bank, dstoff) in ((half, ST_M + half * 512), (2 + half, ST_T + half * 512)):
                        op = ALU.add
                        if first:
                            ff = lambda v, b=bank, o=dstoff: v.tensor_copy(
                                stats[0:1, o:o + 512], pD[b][0:1, 0:512])
                        else:
                            ff = lambda v, b=bank, o=dstoff: v.tensor_tensor(
                                stats[0:1, o:o + 512], stats[0:1, o:o + 512], pD[b][0:1, 0:512], ALU.add)
                        if half == 1 and dstoff >= ST_T:
                            inc(ph, "vector", "ve", ff)
                        else:
                            ph.em("vector", ff)
                first = False
                stat_acc.append(C["ve"])
            wait(ph, "vector", "ve", C["ve"])
            ph.play(nc)

        with ExitStack() as ctx:
            e9 = ctx.enter_context
            pB1 = e9(nc.psum_tensor("pB1f", [128, TOK], F32))
            pB2 = e9(nc.psum_tensor("pB2f", [128, TOK], F32))
            ph = Script()
            ph.em("vector", lambda v: v.tensor_scalar(stats[0:1, ST_M:ST_M + TOK],
                                                      stats[0:1, ST_M:ST_M + TOK], INV_D, None, ALU.mult))
            inc(ph, "vector", "ve",
                lambda v: v.tensor_scalar(stats[0:1, ST_T:ST_T + TOK],
                                          stats[0:1, ST_T:ST_T + TOK], INV_D, None, ALU.mult))
            finalize_rows(ph, pB1, pB2)
            rbrdy = C["se"]
            ve0 = C["ve"]
            gld0 = C["gld"]
            for c in range(NCH):
                slot = XC0 if c % 2 == 0 else XC1
                if c >= 2:
                    wait(ph, "sync", "ve", ve0 + c - 1)
                ld = dma(ph, "sync", "sld", SM[:, slot:slot + TOK], x1_hbm[c * 128:(c + 1) * 128, :])
                wait(ph, "vector", "sld", ld)
                if c == 0:
                    wait(ph, "vector", "se", rbrdy)
                if c >= 2:
                    wait(ph, "vector", "gld", gld0 + 16 * (c - 1))
                ph.em("vector", lambda v, c=c: v.tensor_tensor(
                    SM[:, SQ:SQ + TOK], R2[:, c * TOK:(c + 1) * TOK], rb[:, :], ALU.mult))
                ph.em("vector", lambda v: v.tensor_tensor(
                    SM[:, SQ:SQ + TOK], SM[:, SQ:SQ + TOK], mrb[:, :], ALU.subtract))
                inc(ph, "vector", "ve",
                    lambda v, c=c, s=slot: v.tensor_tensor(
                        SFo[:, (c % 2) * TOK:(c % 2 + 1) * TOK], SM[:, SQ:SQ + TOK],
                        SM[:, s:s + TOK], ALU.add))
                wait(ph, "gpsimd", "ve", C["ve"])
                dma(ph, "gpsimd", "gld", outT[c * 128:(c + 1) * 128, :],
                    SFo[:, (c % 2) * TOK:(c % 2 + 1) * TOK])
            wait(ph, "gpsimd", "gld", C["gld"])
            wait(ph, "sync", "gld", C["gld"])
            ph.play(nc)
    return nc


def kernel(L0_attn_x, L0_attn_w1, L0_attn_w2, L0_attn_b2, L0_attn_sparse_mask,
           L0_ffn_gate_qweight, L0_ffn_gate_scales, L0_ffn_gate_zeros,
           L0_ffn_up_qweight, L0_ffn_up_scales, L0_ffn_up_zeros,
           L0_ffn_down_qweight, L0_ffn_down_scales, L0_ffn_down_zeros):
    x = np.ascontiguousarray(np.asarray(L0_attn_x, dtype=np.float32).reshape(8192, D).T)
    w1 = np.ascontiguousarray(np.asarray(L0_attn_w1, dtype=np.float32))
    w2T = np.ascontiguousarray(np.asarray(L0_attn_w2, dtype=np.float32).T)
    maskA = np.ascontiguousarray(np.asarray(L0_attn_sparse_mask, dtype=np.float32))
    b2t = np.ascontiguousarray(np.asarray(L0_attn_b2, dtype=np.float32).reshape(NCH, 128).T)
    qgA = np.ascontiguousarray(np.asarray(L0_ffn_gate_qweight, dtype=np.int32))
    quA = np.ascontiguousarray(np.asarray(L0_ffn_up_qweight, dtype=np.int32))
    qdA = np.ascontiguousarray(np.asarray(L0_ffn_down_qweight, dtype=np.int32))
    relay = lambda a: np.ascontiguousarray(
        np.asarray(a, np.float32).reshape(NIT, 128, 32).transpose(1, 0, 2).reshape(128, NIT * 32))
    relayd = lambda a: np.ascontiguousarray(
        np.asarray(a, np.float32).reshape(NCH, 128, NIT).transpose(1, 0, 2).reshape(128, NCH * NIT))
    ident = np.eye(128, dtype=np.float32).astype(BF16NP)
    onescA = np.ones((128, 1), np.float32).astype(BF16NP)
    onesrA = np.ones((1, 128), np.float32).astype(BF16NP)

    nc = bass.Bass("TRN2", target_bir_lowering=False)
    build(nc)

    base = dict(w1=w1, w2T=w2T, mask=maskA, b2t=b2t, qg=qgA, qu=quA, qd=qdA,
                sg=relay(L0_ffn_gate_scales), zg=relay(L0_ffn_gate_zeros),
                su=relay(L0_ffn_up_scales), zu=relay(L0_ffn_up_zeros),
                sd=relayd(L0_ffn_down_scales), zd=relayd(L0_ffn_down_zeros),
                ident=ident, onesc=onescA, onesr=onesrA,
                fct=np.array([[EPS, -0.5]], dtype=np.float32))
    in_maps = []
    for c in range(8):
        m = dict(base)
        m["xT"] = np.ascontiguousarray(x[:, c * TOK:(c + 1) * TOK])
        in_maps.append(m)
    res = run_bass_kernel_spmd(nc, in_maps, core_ids=list(range(8)))
    shards = [res.results[c]["outT"] for c in range(8)]
    full = np.concatenate(shards, axis=1)
    return np.ascontiguousarray(full.T).reshape(2, D, D).astype(np.float32)
